# version 61
# baseline (speedup 1.0000x reference)
import numpy as np
import concourse.bass as bass
import concourse.mybir as mybir
from concourse.bass_utils import run_bass_kernel_spmd

F32 = mybir.dt.float32
BF16 = mybir.dt.bfloat16
ALU = mybir.AluOpType
AF = mybir.ActivationFunctionType

D = 1024
KC = 8
E = 8
H = 128


class Sem:
    def __init__(self, stack, nc, name):
        self.h = stack.enter_context(nc.semaphore(name))
        self.n = 0

    def inc(self, instr, amt=1):
        instr.then_inc(self.h, amt)
        self.n += amt
        return self.n


def build_program(T=2048, TC=512):
    NCH = T // TC
    nc = bass.Bass("TRN2", target_bir_lowering=False)

    xT = nc.dram_tensor("xT", [D, T], F32, kind="ExternalInput").ap()
    wqT = nc.dram_tensor("wqT", [D, 3 * D], F32, kind="ExternalInput").ap()
    gamT = nc.dram_tensor("gamT", [128, KC], F32, kind="ExternalInput").ap()
    w1p = nc.dram_tensor("w1p", [D, E * H], F32, kind="ExternalInput").ap()
    b1T = nc.dram_tensor("b1T", [H, E], F32, kind="ExternalInput").ap()
    w2p = nc.dram_tensor("w2p", [E * H, D], F32, kind="ExternalInput").ap()
    b2 = nc.dram_tensor("b2", [E, D], F32, kind="ExternalInput").ap()
    wgp = nc.dram_tensor("wgp", [128, KC * 128], F32, kind="ExternalInput").ap()
    bg = nc.dram_tensor("bg", [E, 1], F32, kind="ExternalInput").ap()
    selc = nc.dram_tensor("selc", [E, E * 128], F32, kind="ExternalInput").ap()
    outT = nc.dram_tensor("outT", [D, T], F32, kind="ExternalOutput").ap()

    xT_v = xT.rearrange("(a p) t -> p a t", p=128)
    outT_v = outT.rearrange("(a p) t -> p a t", p=128)
    wqT_v = wqT.rearrange("(a p) e -> p a e", p=128)
    w1p_v = w1p.rearrange("(a p) f -> p a f", p=128)
    w2p_v = w2p.rearrange("(e p) d -> p e d", p=128)

    import contextlib
    stack = contextlib.ExitStack()
    with stack:
        sb = lambda name, shape, dt: stack.enter_context(nc.sbuf_tensor(name, shape, dt))
        wq_bf = sb("wq_bf", [128, KC, 3 * D], BF16)
        w1_bf = sb("w1_bf", [128, KC, E * H], BF16)
        w2_bf = sb("w2_bf", [128, E, D], BF16)
        wg_bf = sb("wg_bf", [128, KC, 128], BF16)
        b2_bf = sb("b2_bf", [E, D], BF16)
        gam = sb("gam", [128, KC], F32)
        b1s = sb("b1s", [H, E], F32)
        bgs = sb("bgs", [E, 1], F32)
        ones_bf = sb("ones_bf", [128, 128], BF16)
        ones_f = sb("ones_f", [128, 128], F32)
        statc = sb("statc", [128, 1], BF16)
        sel_bf = sb("sel_bf", [E, E * 128], BF16)

        x_sb = sb("x_sb", [128, KC, TC], F32)
        xbf = sb("xbf", [128, KC, TC], BF16)
        rm_bf = sb("rm_bf", [128, 2, TC], BF16)
        rm2_bf = sb("rm2_bf", [128, 2, TC], BF16)
        tbf = sb("tbf", [128, KC, TC], BF16)
        xsq = tbf
        hbf = sb("hbf", [128, KC, TC], BF16)
        y_sb = sb("y_sb", [128, KC, TC], F32)
        x2_sb = sb("x2_sb", [128, KC, TC], F32)
        x2_sb2 = sb("x2_sb2", [128, KC, TC], F32)
        asig = sb("asig", [128, 2, TC], F32)
        h1g = sb("h1g", [128, E, TC], BF16)
        carry = sb("carry", [128, KC], F32)
        rows = sb("rows", [1, 6 * TC], F32)
        plscr = sb("plscr", [1, 1], F32)
        rows_bf = sb("rows_bf", [1, TC], BF16)
        exg = sb("exg", [8, TC], BF16)
        g_bf = sb("g_bf", [8, TC], BF16)

        def x2buf(c):
            return x2_sb if c % 2 == 0 else x2_sb2

        def row(i):
            return rows[0:1, i * TC:(i + 1) * TC]
        R_M, R_R, R_T0, R_S2, R_INV, R_GI = 0, 1, 2, 3, 4, 5

        def rowbf(i):
            return rows_bf[0:1, i * TC:(i + 1) * TC]

        ps = lambda name, shape: stack.enter_context(nc.psum_tensor(name, shape, F32))
        ps_kv = ps("ps_kv", [128, 2, TC])
        ps_a = ps("ps_a", [128, 2, TC])
        ps_rep = ps("ps_rep", [128, 2, TC])
        ps_q = ps("ps_q", [128, 2, TC])

        s_lds = Sem(stack, nc, "s_lds")
        s_wq = [Sem(stack, nc, f"s_wq{g}") for g in range(3)]
        s_ldx = Sem(stack, nc, "s_ldx")
        s_ldw = Sem(stack, nc, "s_ldw")
        s_ldxb = Sem(stack, nc, "s_ldxb")
        s_ldxh = Sem(stack, nc, "s_ldxh")
        s_st = Sem(stack, nc, "s_st")
        s_mm = Sem(stack, nc, "s_mm")
        s_dve = Sem(stack, nc, "s_dve")
        s_act = Sem(stack, nc, "s_act")
        s_gp = Sem(stack, nc, "s_gp")

        block = stack.enter_context(nc.Block())

        class _MockInstr:
            def then_inc(self, sem, amt=1):
                return self

        class _MockEng:
            def __getattr__(self, name):
                def f(*a, **k):
                    return _MockInstr()
                return f

        state = {"counting": True, "on": None}

        def on(eng, fn):
            state["on"](eng, fn)

        def rec(d, key, sem):
            if state["counting"]:
                d[key] = sem.n

        def rec_i(lst, i, sem):
            if state["counting"]:
                lst[i] = sem.n

        def init(d, key):
            if state["counting"]:
                d[key] = [0] * KC

        ck = [dict() for _ in range(NCH)]

        G = {}

        def run_preamble():
            def _pre_sync(sync):
                sync.dma_start(out=gam[:], in_=gamT).then_inc(s_lds.h, 16)
                sync.dma_start(out=b1s[:], in_=b1T).then_inc(s_lds.h, 16)
                sync.dma_start(out=bgs[:], in_=bg).then_inc(s_lds.h, 16)
                sync.dma_start(out=x_sb[:, 0:4, :],
                               in_=xT_v[:, 0:4, 0:TC]).then_inc(s_ldxh.h, 16)
                sync.dma_start(out=x_sb[:, 4:8, :],
                               in_=xT_v[:, 4:8, 0:TC]).then_inc(s_ldx.h, 16)
            on("sync", _pre_sync)
            s_lds.n += 48
            s_ldx.n += 16
            s_ldxh.n += 16
            if state["counting"]:
                G["ld_small"] = s_lds.n
                ck[0]["ld_x"] = s_ldx.n
                G["ld_xh"] = s_ldxh.n

            def _pre_gp(gpsimd):
                for g, lo in ((1, 1024), (2, 2048), (0, 0)):
                    gpsimd.dma_start(out=wq_bf[:, :, lo:lo + 1024],
                                     in_=wqT_v[:, :, lo:lo + 1024]).then_inc(s_wq[g].h, 16)
                gpsimd.dma_start(out=w1_bf[:], in_=w1p_v).then_inc(s_ldw.h, 16)
                gpsimd.dma_start(out=w2_bf[:], in_=w2p_v).then_inc(s_ldw.h, 16)
                gpsimd.dma_start(out=wg_bf[:], in_=wgp).then_inc(s_ldw.h, 16)
                gpsimd.dma_start(out=b2_bf[:], in_=b2).then_inc(s_ldw.h, 16)
                gpsimd.dma_start(out=sel_bf[:], in_=selc).then_inc(s_ldw.h, 16)
            on("gpsimd", _pre_gp)
            for g in range(3):
                s_wq[g].n += 16
            s_ldw.n += 16 * 5
            if state["counting"]:
                G["ld_w"] = s_ldw.n

            def _pre_dve(vector):
                vector.memset(statc[:], 1.0 / D)
                vector.memset(plscr[:], 1.0)
                vector.memset(ones_bf[:], 1.0)
                ii = vector.memset(ones_f[:], 1.0)
                s_dve.inc(ii)
                vector.wait_ge(s_lds.h, G["ld_small"])
            on("vector", _pre_dve)
            if state["counting"]:
                G["memset"] = s_dve.n

        def emitFolds(g):
            def _fold(vector, g=g):
                vector.wait_ge(s_wq[g].h, 16)
                for k in range(KC):
                    ii = vector.tensor_scalar_mul(
                        wq_bf[:, k, g * 1024:(g + 1) * 1024],
                        wq_bf[:, k, g * 1024:(g + 1) * 1024],
                        gam[:, k:k + 1])
                s_dve.inc(ii)
            on("vector", _fold)
            if state["counting"]:
                G[f"fold{g}"] = s_dve.n

        def emit_preload(func):
            def _pl(scalar, func=func):
                scalar.wait_ge(s_act.h, s_act.n)
                ii = scalar.activation(plscr[:], plscr[:], func)
                s_act.inc(ii)
            on("scalar", _pl)

        def emit_rows(K, key, eps_add, key_m=None):
            def _rows(vector):
                vector.wait_ge(s_mm.h, K[key_m or key])
                s_dve.inc(vector.tensor_copy(row(R_M), ps_q[0:1, 0, :]))
                rec(K, key + "_rdm", s_dve)
                vector.wait_ge(s_dve.h, s_dve.n)
                s_dve.inc(vector.tensor_copy(rowbf(0), row(R_M)))
                s_dve.inc(vector.tensor_mul(row(R_T0), row(R_M), row(R_M)))
                vector.wait_ge(s_dve.h, s_dve.n)
                vector.wait_ge(s_mm.h, K[key])
                s_dve.inc(vector.scalar_tensor_tensor(row(R_S2), row(R_T0), -1.0,
                                                      ps_q[0:1, 1, :], ALU.mult,
                                                      ALU.add))
                rec(K, key + "_rd", s_dve)
                vector.wait_ge(s_dve.h, s_dve.n)
                if eps_add is None:
                    s_dve.inc(vector.tensor_scalar_max(row(R_S2), row(R_S2), 1e-33))
                else:
                    s_dve.inc(vector.tensor_scalar_add(row(R_S2), row(R_S2), eps_add))
                vector.wait_ge(s_dve.h, s_dve.n)
                ii = vector.reciprocal(row(R_INV), row(R_S2))
                s_dve.inc(ii)
            on("vector", _rows)
            rec(K, key + "_row1", s_dve)

            def _r(scalar):
                scalar.wait_ge(s_dve.h, K[key + "_row1"])
                ii = scalar.activation(row(R_R), row(R_INV), AF.Sqrt)
                s_act.inc(ii)
            on("scalar", _r)
            rec(K, key + "_actr", s_act)

        def emitLoadX(c):
            K = ck[c]
            tsl = slice(c * TC, (c + 1) * TC)

            def _ld(sync):
                sync.wait_ge(s_gp.h, ck[c - 1]["x2"][KC - 1])
                sync.dma_start(out=x_sb[:], in_=xT_v[:, :, tsl]).then_inc(s_ldx.h, 16)
            on("sync", _ld)
            s_ldx.n += 16
            rec(K, "ld_x", s_ldx)

        def emitLoadXbf(c):
            K = ck[c]
            tsl = slice(c * TC, (c + 1) * TC)

            def _ldbf(gpsimd):
                gpsimd.wait_ge(s_dve.h, ck[c - 1]["dve_h"])
                if c == 1:
                    gpsimd.wait_ge(s_act.h, ck[0]["act_sq0"])
                else:
                    gpsimd.wait_ge(s_gp.h, ck[c - 1]["gp_sq"])
                gpsimd.dma_start(out=xbf[:], in_=xT_v[:, :, tsl]).then_inc(s_ldxb.h, 16)
            on("gpsimd", _ldbf)
            s_ldxb.n += 16
            rec(K, "ld_xbf", s_ldxb)

        def emitS1(c):
            K = ck[c]

            def _s1(tensor):
                if c == 0:
                    tensor.wait_ge(s_dve.h, G["memset"])
                else:
                    tensor.wait_ge(s_ldxb.h, K["ld_xbf"])
                    tensor.wait_ge(s_dve.h, ck[c - 1]["mm_s2b_rdm"])
                for k in range(KC):
                    if c == 0:
                        if k < 4:
                            tensor.wait_ge(s_act.h, K["cast_h"])
                        else:
                            tensor.wait_ge(s_act.h, K["cast_f"])
                    ii = tensor.matmul(ps_q[0:1, 0, :], statc[:], xbf[:, k, :],
                                       start=(k == 0), stop=(k == KC - 1))
                s_mm.inc(ii)
            on("tensor", _s1)
            rec(K, "mm_s1", s_mm)

        def emitSq(c):
            K = ck[c]
            if c == 0:
                def _sq0(scalar):
                    scalar.wait_ge(s_ldxh.h, G["ld_xh"])
                    s_act.inc(scalar.activation(xbf[:, 0:4, :], x_sb[:, 0:4, :],
                                                AF.Copy))
                    scalar.wait_ge(s_act.h, s_act.n)
                    s_act.inc(scalar.activation(xsq[:, 0:4, :], xbf[:, 0:4, :],
                                                AF.Square))
                    scalar.wait_ge(s_ldx.h, K["ld_x"])
                    s_act.inc(scalar.activation(xbf[:, 4:8, :], x_sb[:, 4:8, :],
                                                AF.Copy))
                    scalar.wait_ge(s_act.h, s_act.n)
                    ii = scalar.activation(xsq[:, 4:8, :], xbf[:, 4:8, :],
                                           AF.Square)
                    s_act.inc(ii)
                on("scalar", _sq0)
                if state["counting"]:
                    K["cast_h"] = s_act.n - 3
                    K["act_sq0h"] = s_act.n - 2
                    K["cast_f"] = s_act.n - 1
                rec(K, "act_sq0", s_act)
                return

            def _sq(gpsimd):
                gpsimd.wait_ge(s_ldxb.h, K["ld_xbf"])
                gpsimd.wait_ge(s_dve.h, ck[c - 1]["dve_h"])
                gpsimd.wait_ge(s_mm.h, ck[c - 1]["mm_s2b"])
                ii = gpsimd.tensor_tensor(xsq[:], xbf[:], xbf[:], ALU.mult)
                s_gp.inc(ii)
            on("gpsimd", _sq)
            rec(K, "gp_sq", s_gp)

        def emitS2(c):
            K = ck[c]

            def _s2(tensor):
                if c != 0:
                    tensor.wait_ge(s_gp.h, K["gp_sq"])
                    tensor.wait_ge(s_dve.h, ck[c - 1]["dve_invs"])
                for k in range(KC):
                    if c == 0:
                        tensor.wait_ge(s_act.h,
                                       K["act_sq0h"] if k < 4 else K["act_sq0"])
                    ii = tensor.matmul(ps_q[0:1, 1, :], statc[:], xsq[:, k, :],
                                       start=(k == 0), stop=(k == KC - 1))
                s_mm.inc(ii)
            on("tensor", _s2)
            rec(K, "mm_s2", s_mm)

        def emitRows1(c, pl=True):
            emit_rows(ck[c], "mm_s2", eps_add=None, key_m="mm_s1")
            if pl:
                emit_preload(AF.Sigmoid)

        def emitRep1(c):
            K = ck[c]

            def _rep(tensor):
                tensor.wait_ge(s_act.h, K["mm_s2_actr"])
                tensor.matmul(ps_q[:, 0, :], ones_f[0:1, :], row(R_R),
                              start=True, stop=True)
                ii = tensor.matmul(ps_q[:, 1, :], ones_bf[0:1, :], rowbf(0),
                                   start=True, stop=True)
                s_mm.inc(ii)
            on("tensor", _rep)
            rec(K, "mm_rep1", s_mm)

            def _ev(scalar):
                scalar.wait_ge(s_mm.h, K["mm_rep1"])
                ii = scalar.activation(rm_bf[:, :, :], ps_q[:, :, :], AF.Copy)
                s_act.inc(ii)
            on("scalar", _ev)
            rec(K, "evac1", s_act)

        def emit_h_mul(c, ks, first):
            K = ck[c]

            def _m(vector):
                if first:
                    vector.wait_ge(s_act.h, K["evac1"])
                    vector.wait_ge(s_ldxb.h, K["ld_xbf"])
                for k in ks:
                    ii = vector.tensor_tensor(tbf[:, k, :], xbf[:, k, :],
                                              rm_bf[:, 1, :], ALU.subtract)
                    s_dve.inc(ii)
            on("vector", _m)

        def emitPairs2(c, cprev):
            K = ck[c]
            init(K, "pair")

            def _s(vector):
                vector.wait_ge(s_dve.h, s_dve.n)
                for k in range(KC):
                    if cprev is not None:
                        vector.wait_ge(s_mm.h, ck[cprev]["h1k"][k])
                    ii = vector.tensor_tensor(hbf[:, k, :], tbf[:, k, :],
                                              rm_bf[:, 0, :], ALU.mult)
                    s_dve.inc(ii)
                    rec_i(K["pair"], k, s_dve)
            on("vector", _s)
            rec(K, "dve_h", s_dve)

        def emitB(c):
            K = ck[c]
            init(K, "mm_kv")
            init(K, "mm_a")
            init(K, "sig")
            init(K, "scan")
            init(K, "mm_q")
            init(K, "attn")
            init(K, "x2")

            for j in range(KC):
                def _kv(tensor, j=j):
                    if j == 0:
                        if c == 0:
                            tensor.wait_ge(s_dve.h, G["fold1"])
                        if c > 0:
                            tensor.wait_ge(s_act.h, ck[c - 1]["relu"][6])
                    if j == 1 and c > 0:
                        tensor.wait_ge(s_act.h, ck[c - 1]["relu"][7])
                    if j >= 2:
                        tensor.wait_ge(s_dve.h, K["scan"][j - 2])
                    for k in range(KC):
                        if j == 0:
                            tensor.wait_ge(s_dve.h, K["pair"][k])
                        ii = tensor.matmul(ps_kv[:, j % 2, :],
                                           wq_bf[:, k, (8 + j) * 128:(9 + j) * 128],
                                           hbf[:, k, :],
                                           start=(k == 0), stop=(k == KC - 1))
                    s_mm.inc(ii)
                on("tensor", _kv)
                rec_i(K["mm_kv"], j, s_mm)

                def _a(tensor, j=j):
                    if j == 0 and c == 0:
                        tensor.wait_ge(s_dve.h, G["fold2"])
                    if j == 0 and c > 0:
                        tensor.wait_ge(s_dve.h, ck[c - 1]["dve_g"])
                        tensor.wait_ge(s_act.h, ck[c - 1]["relu"][4])
                    if j == 1 and c > 0:
                        tensor.wait_ge(s_act.h, ck[c - 1]["relu"][5])
                    if j >= 2:
                        tensor.wait_ge(s_act.h, K["sig"][j - 2])
                    for k in range(KC):
                        ii = tensor.matmul(ps_a[:, j % 2, :],
                                           wq_bf[:, k, (16 + j) * 128:(17 + j) * 128],
                                           hbf[:, k, :],
                                           start=(k == 0), stop=(k == KC - 1))
                    s_mm.inc(ii)
                on("tensor", _a)
                rec_i(K["mm_a"], j, s_mm)

                def _sig(scalar, j=j):
                    scalar.wait_ge(s_mm.h, K["mm_a"][j])
                    if c > 0 and j < 2:
                        scalar.wait_ge(s_gp.h, ck[c - 1]["gp_ln2"])
                    ii = scalar.activation(asig[:, j % 2, :], ps_a[:, j % 2, :],
                                           AF.Sigmoid)
                    s_act.inc(ii)
                on("scalar", _sig)
                rec_i(K["sig"], j, s_act)
                if j == KC - 1:
                    emit_preload(AF.Sqrt)

                def _scan(vector, j=j):
                    vector.wait_ge(s_act.h, K["sig"][j])
                    init = 0.0 if c == 0 else carry[:, j:j + 1]
                    s_dve.inc(vector.tensor_tensor_scan(y_sb[:, j, :], asig[:, j % 2, :],
                                                        ps_kv[:, j % 2, :], init,
                                                        ALU.mult, ALU.add))
                    vector.wait_ge(s_dve.h, s_dve.n)
                    ii = vector.tensor_copy(carry[:, j:j + 1], y_sb[:, j, TC - 1:TC])
                    s_dve.inc(ii)
                on("vector", _scan)
                rec_i(K["scan"], j, s_dve)

                if j >= 2:
                    emit_q(c, j - 2)
            emit_q(c, KC - 2)
            emit_q(c, KC - 1)

        def emit_q(c, j):
            K = ck[c]
            if True:
                def _q(tensor, j=j):
                    if j < 2:
                        tensor.wait_ge(s_dve.h, K["dve_h"])
                        tensor.wait_ge(s_act.h, K["evac1"])
                        if c == 0:
                            tensor.wait_ge(s_dve.h, G["fold0"])
                    if j >= 2:
                        tensor.wait_ge(s_dve.h, K["attn"][j - 2])
                    for k in range(KC):
                        ii = tensor.matmul(ps_q[:, j % 2, :],
                                           wq_bf[:, k, j * 128:(j + 1) * 128],
                                           hbf[:, k, :],
                                           start=(k == 0), stop=(k == KC - 1))
                    s_mm.inc(ii)
                on("tensor", _q)
                rec_i(K["mm_q"], j, s_mm)

                def _attn(vector, j=j):
                    vector.wait_ge(s_mm.h, K["mm_q"][j])
                    vector.wait_ge(s_dve.h, K["scan"][j])
                    ii = vector.tensor_tensor(y_sb[:, j, :], ps_q[:, j % 2, :],
                                              y_sb[:, j, :], ALU.mult)
                    s_dve.inc(ii)
                on("vector", _attn)
                rec_i(K["attn"], j, s_dve)

                def _x2(gpsimd, j=j):
                    gpsimd.wait_ge(s_dve.h, K["attn"][j])
                    if j == 0:
                        gpsimd.wait_ge(s_ldx.h, K["ld_x"])
                        if c == 0:
                            gpsimd.wait_ge(s_ldxh.h, G["ld_xh"])
                    if j == 0 and c > 1:
                        gpsimd.wait_ge(s_st.h, ck[c - 2]["st"])
                    ii = gpsimd.tensor_tensor(x2buf(c)[:, j, :], x_sb[:, j, :],
                                              y_sb[:, j, :], ALU.add)
                    s_gp.inc(ii)
                on("gpsimd", _x2)
                rec_i(K["x2"], j, s_gp)

        def emitC(c):
            K = ck[c]
            init(K, "cast")

            for j in range(KC):
                def _cast(scalar, j=j):
                    scalar.wait_ge(s_gp.h, K["x2"][j])
                    if j == 0:
                        scalar.wait_ge(s_mm.h, K["mm_q"][KC - 1])
                    ii = scalar.activation(hbf[:, j, :], x2buf(c)[:, j, :], AF.Copy)
                    s_act.inc(ii)
                on("scalar", _cast)
                rec_i(K["cast"], j, s_act)

            init(K, "mm_s1bk")
            init(K, "sq2")

            def _s1(tensor):
                tensor.wait_ge(s_dve.h, K["attn"][KC - 1])
                for k in range(KC):
                    tensor.wait_ge(s_act.h, K["cast"][k])
                    ii = tensor.matmul(ps_q[0:1, 0, :], statc[:], hbf[:, k, :],
                                       start=(k == 0), stop=(k == KC - 1))
                    s_mm.inc(ii)
                    rec_i(K["mm_s1bk"], k, s_mm)
            on("tensor", _s1)
            rec(K, "mm_s1b", s_mm)

            def _sq(scalar):
                for k in range(KC):
                    scalar.wait_ge(s_mm.h, K["mm_s1bk"][k])
                    ii = scalar.activation(tbf[:, k, :], hbf[:, k, :], AF.Square)
                    s_act.inc(ii)
                    rec_i(K["sq2"], k, s_act)
            on("scalar", _sq)

            def _s2(tensor):
                for k in range(KC):
                    tensor.wait_ge(s_act.h, K["sq2"][k])
                    ii = tensor.matmul(ps_q[0:1, 1, :], statc[:], tbf[:, k, :],
                                       start=(k == 0), stop=(k == KC - 1))
                s_mm.inc(ii)
            on("tensor", _s2)
            rec(K, "mm_s2b", s_mm)

            emit_rows(K, "mm_s2b", eps_add=1e-5, key_m="mm_s1b")
            emit_preload(AF.Exp)

        def emitRep2(c):
            K = ck[c]

            def _rep(tensor):
                tensor.wait_ge(s_act.h, K["mm_s2b_actr"])
                if c > 0:
                    tensor.wait_ge(s_dve.h, ck[c - 1]["h1g"][E - 1])
                tensor.matmul(ps_rep[:, 0, :], ones_f[0:1, :], row(R_R),
                              start=True, stop=True)
                ii = tensor.matmul(ps_rep[:, 1, :], ones_bf[0:1, :], rowbf(0),
                                   start=True, stop=True)
                s_mm.inc(ii)
            on("tensor", _rep)
            rec(K, "mm_rep2", s_mm)

            def _evg(vector):
                vector.wait_ge(s_mm.h, K["mm_rep2"])
                vector.tensor_copy(asig[:, 0, :], ps_rep[:, 0, :])
                ii = vector.tensor_copy(asig[:, 1, :], ps_rep[:, 1, :])
                s_dve.inc(ii)
            on("vector", _evg)
            rec(K, "dve_evac", s_dve)

        def emitLn2(c):
            K = ck[c]
            init(K, "ln2k")
            NDVE = 5
            for k in range(NDVE):
                def _ln2(vector, k=k):
                    s_dve.inc(vector.tensor_tensor(y_sb[:, k, :], x2buf(c)[:, k, :],
                                                   ps_rep[:, 1, :], ALU.subtract))
                    vector.wait_ge(s_dve.h, s_dve.n)
                    ii = vector.tensor_tensor(hbf[:, k, :], y_sb[:, k, :],
                                              ps_rep[:, 0, :], ALU.mult)
                    s_dve.inc(ii)
                on("vector", _ln2)
                rec_i(K["ln2k"], k, s_dve)
            for k in range(NDVE, KC):
                def _ln2g(gpsimd, k=k):
                    if k == NDVE:
                        gpsimd.wait_ge(s_dve.h, K["dve_evac"])
                    s_gp.inc(gpsimd.tensor_tensor(y_sb[:, k, :], x2buf(c)[:, k, :],
                                                  asig[:, 1, :], ALU.subtract))
                    gpsimd.wait_ge(s_gp.h, s_gp.n)
                    ii = gpsimd.tensor_tensor(hbf[:, k, :], y_sb[:, k, :],
                                              asig[:, 0, :], ALU.mult)
                    s_gp.inc(ii)
                on("gpsimd", _ln2g)
                rec_i(K["ln2k"], k, s_gp)
            if state["counting"]:
                K["dve_ln2"] = K["ln2k"][NDVE - 1]
                K["gp_ln2"] = K["ln2k"][KC - 1]
            K["n_dve_ln2"] = NDVE

        def emitScores(c):
            K = ck[c]

            def _sc(tensor):
                if c == 0:
                    tensor.wait_ge(s_ldw.h, G["ld_w"])
                if c > 0:
                    tensor.wait_ge(s_dve.h, ck[c - 1]["out"][2])
                for k in range(KC):
                    if k < K["n_dve_ln2"]:
                        tensor.wait_ge(s_dve.h, K["ln2k"][k])
                    else:
                        tensor.wait_ge(s_gp.h, K["ln2k"][k])
                    ii = tensor.matmul(ps_a[:, 0, :], wg_bf[:, k, :], hbf[:, k, :],
                                       start=(k == 0), stop=(k == KC - 1))
                s_mm.inc(ii)
            on("tensor", _sc)
            rec(K, "mm_sc", s_mm)

            def _exp(scalar):
                scalar.wait_ge(s_mm.h, K["mm_sc"])
                ii = scalar.activation(exg[:], ps_a[0:8, 0, :], AF.Exp, bias=bgs[:])
                s_act.inc(ii)
            on("scalar", _exp)
            rec(K, "act_exp", s_act)

        def emitSum(c):
            K = ck[c]

            def _sum(tensor):
                tensor.wait_ge(s_act.h, K["act_exp"])
                ii = tensor.matmul(ps_q[0:1, 1, :], ones_bf[0:8, 0:1], exg[:],
                                   start=True, stop=True)
                s_mm.inc(ii)
            on("tensor", _sum)
            rec(K, "mm_sum", s_mm)

        def emitSumInv(c):
            K = ck[c]

            def _inv(vector):
                vector.wait_ge(s_mm.h, K["mm_sum"])
                ii = vector.reciprocal(row(R_GI), ps_q[0:1, 1, :])
                s_dve.inc(ii)
            on("vector", _inv)
            rec(K, "dve_invs", s_dve)

        def emitDtail(c):
            K = ck[c]

            def _invrep(tensor):
                tensor.wait_ge(s_dve.h, K["dve_invs"])
                if c > 0:
                    tensor.wait_ge(s_dve.h, ck[c - 1]["out"][E - 1])
                ii = tensor.matmul(ps_a[0:8, 1, :], ones_f[0:1, 0:8], row(R_GI),
                                   start=True, stop=True)
                s_mm.inc(ii)
            on("tensor", _invrep)
            rec(K, "mm_invr", s_mm)

            def _g(vector):
                vector.wait_ge(s_mm.h, K["mm_invr"])
                if c > 0:
                    vector.wait_ge(s_mm.h, ck[c - 1]["mm_y2"][E - 1])
                ii = vector.tensor_tensor(g_bf[:], exg[:], ps_a[0:8, 1, :], ALU.mult)
                s_dve.inc(ii)
            on("vector", _g)
            rec(K, "dve_g", s_dve)
            emit_preload(AF.Sqrt)

        def emitE1(c):
            K = ck[c]
            init(K, "mm_h1")
            init(K, "h1k")
            init(K, "mm_gr")
            init(K, "relu")
            init(K, "h1g")

            BM = (0, 1, 0, 1, 2, 3, 0, 1)

            for e in range(E):
                def _h1(tensor, e=e):
                    if e == 0:
                        if c > 0:
                            tensor.wait_ge(s_dve.h, ck[c - 1]["out"][4])
                    if e == 1 and c > 0:
                        tensor.wait_ge(s_dve.h, ck[c - 1]["out"][5])
                    if e == 2:
                        tensor.wait_ge(s_act.h, K["relu"][0])
                    if e == 3:
                        tensor.wait_ge(s_act.h, K["relu"][1])
                    if e == 4 and c > 0:
                        tensor.wait_ge(s_dve.h, ck[c - 1]["out"][6])
                    if e == 5 and c > 0:
                        tensor.wait_ge(s_dve.h, ck[c - 1]["out"][E - 1])
                    if e == 4:
                        tensor.wait_ge(s_act.h, K["act_exp"])
                    if e == 5:
                        tensor.wait_ge(s_dve.h, K["dve_g"])
                    if e == 6:
                        tensor.wait_ge(s_act.h, K["relu"][2])
                    if e == 7:
                        tensor.wait_ge(s_act.h, K["relu"][3])
                    pb = ps_kv if BM[e] < 2 else ps_a
                    for k in range(KC):
                        if e == 0:
                            if k < K["n_dve_ln2"]:
                                tensor.wait_ge(s_dve.h, K["ln2k"][k])
                            else:
                                tensor.wait_ge(s_gp.h, K["ln2k"][k])
                        ii = tensor.matmul(pb[:, BM[e] % 2, :],
                                           w1_bf[:, k, e * 128:(e + 1) * 128],
                                           hbf[:, k, :],
                                           start=(k == 0), stop=(k == KC - 1))
                        if e == E - 1:
                            s_mm.inc(ii)
                            rec_i(K["h1k"], k, s_mm)
                    if e != E - 1:
                        s_mm.inc(ii)
                on("tensor", _h1)
                rec_i(K["mm_h1"], e, s_mm)

                if e == 2:
                    emitDtail(c)

                    def _gr01(tensor):
                        tensor.wait_ge(s_dve.h, K["dve_g"])
                        for ee in range(2):
                            ii = tensor.matmul(ps_rep[:, ee % 2, :],
                                               sel_bf[:, ee * 128:(ee + 1) * 128],
                                               g_bf[:], start=True, stop=True)
                            s_mm.inc(ii)
                            rec_i(K["mm_gr"], ee, s_mm)
                    on("tensor", _gr01)

                    def _h1g01(vector):
                        if c > 0:
                            vector.wait_ge(s_mm.h, ck[c - 1]["mm_y2"][E - 1])
                        for ee in range(2):
                            vector.wait_ge(s_act.h, K["relu"][ee])
                            vector.wait_ge(s_mm.h, K["mm_gr"][ee])
                            ii = vector.tensor_tensor(h1g[:, ee, :],
                                                      y_sb[:, ee, :],
                                                      ps_rep[:, ee % 2, :],
                                                      ALU.mult)
                            s_dve.inc(ii)
                            rec_i(K["h1g"], ee, s_dve)
                    on("vector", _h1g01)

                    if c + 1 < NCH:
                        emitSq(c + 1)
                        emitS2(c + 1)
                        emitRows1(c + 1, pl=False)

                if e == 4 and c + 1 < NCH:
                    emitRep1(c + 1)

                if e >= 2:
                    def _gr(tensor, e=e):
                        tensor.wait_ge(s_dve.h, K["h1g"][e - 2])
                        ii = tensor.matmul(ps_rep[:, e % 2, :],
                                           sel_bf[:, e * 128:(e + 1) * 128],
                                           g_bf[:], start=True, stop=True)
                        s_mm.inc(ii)
                    on("tensor", _gr)
                    rec_i(K["mm_gr"], e, s_mm)

                def _relu(scalar, e=e):
                    scalar.wait_ge(s_mm.h, K["mm_h1"][e])
                    pb = ps_kv if BM[e] < 2 else ps_a
                    ii = scalar.activation(y_sb[:, e, :], pb[:, BM[e] % 2, :],
                                           AF.Relu, bias=b1s[:, e:e + 1])
                    s_act.inc(ii)
                on("scalar", _relu)
                rec_i(K["relu"], e, s_act)

                if e >= 5 and c + 1 < NCH:
                    ks = {5: (0, 1, 2), 6: (3, 4, 5), 7: (6, 7)}[e]
                    emit_h_mul(c + 1, ks, first=(e == 5))
                if e == 6 and c + 1 < NCH:
                    emit_preload(AF.Sigmoid)

                if e >= 2 and not (e == E - 1 and c + 1 < NCH):
                    emitH1g(c, e)

        def emitH1g(c, e):
            K = ck[c]

            def _h1g(vector, e=e):
                vector.wait_ge(s_act.h, K["relu"][e])
                vector.wait_ge(s_mm.h, K["mm_gr"][e])
                ii = vector.tensor_tensor(h1g[:, e, :], y_sb[:, e, :],
                                          ps_rep[:, e % 2, :], ALU.mult)
                s_dve.inc(ii)
            on("vector", _h1g)
            rec_i(K["h1g"], e, s_dve)

        def emitY2grp(c, ds, next_c):
            K = ck[c]
            if state["counting"]:
                K.setdefault("mm_y2", [0] * KC)
                K.setdefault("out", [0] * KC)

            for d in ds:
                def _y2(tensor, d=d):
                    if d == 0:
                        tensor.wait_ge(s_dve.h, K["h1g"][E - 1])
                        if next_c is not None:
                            tensor.wait_ge(s_dve.h, ck[next_c]["scan"][KC - 1])
                    if d >= 4:
                        tensor.wait_ge(s_dve.h, K["out"][d - 4])
                    pb = ps_kv if d % 4 < 2 else ps_a
                    tensor.matmul(pb[:, d % 2, :], b2_bf[:, d * 128:(d + 1) * 128],
                                  g_bf[:], start=True, stop=False)
                    for e in range(E):
                        ii = tensor.matmul(pb[:, d % 2, :],
                                           w2_bf[:, e, d * 128:(d + 1) * 128],
                                           h1g[:, e, :],
                                           start=False, stop=(e == E - 1))
                    s_mm.inc(ii)
                on("tensor", _y2)
                rec_i(K["mm_y2"], d, s_mm)

        def emitOuts(c, ds):
            K = ck[c]
            for d in ds:
                def _out(vector, d=d):
                    vector.wait_ge(s_mm.h, K["mm_y2"][d])
                    pb = ps_kv if d % 4 < 2 else ps_a
                    ii = vector.tensor_tensor(x2buf(c)[:, d, :], pb[:, d % 2, :],
                                              x2buf(c)[:, d, :], ALU.add)
                    s_dve.inc(ii)
                on("vector", _out)
                rec_i(K["out"], d, s_dve)

        def emitStore(c):
            K = ck[c]
            tsl = slice(c * TC, (c + 1) * TC)

            def _st(sync):
                sync.wait_ge(s_dve.h, K["out"][E - 1])
                sync.dma_start(out=outT_v[:, :, tsl], in_=x2buf(c)[:]).then_inc(s_st.h, 16)
            on("sync", _st)
            s_st.n += 16
            rec(K, "st", s_st)

        def run_schedule():
            emitSq(0)
            emitS1(0)
            emitS2(0)
            emitRows1(0)
            emitFolds(1)
            emitRep1(0)
            def _boot_pairs(vector):
                vector.wait_ge(s_act.h, ck[0]["evac1"])
                vector.wait_ge(s_act.h, ck[0]["cast_f"])
                if state["counting"]:
                    ck[0]["pair"] = [0] * KC
                for k in range(KC):
                    s_dve.inc(vector.tensor_tensor(tbf[:, k, :], xbf[:, k, :],
                                                   rm_bf[:, 1, :], ALU.subtract))
                    vector.wait_ge(s_dve.h, s_dve.n)
                    ii = vector.tensor_tensor(hbf[:, k, :], tbf[:, k, :],
                                              rm_bf[:, 0, :], ALU.mult)
                    s_dve.inc(ii)
                    rec_i(ck[0]["pair"], k, s_dve)
            on("vector", _boot_pairs)
            rec(ck[0], "dve_h", s_dve)
            emitFolds(2)
            emitFolds(0)
            for c in range(NCH):
                emitB(c)
                if c + 1 < NCH:
                    emitLoadX(c + 1)
                emitC(c)
                if c > 0:
                    emitY2grp(c - 1, [0, 1], c)
                    emitOuts(c - 1, [0, 1])
                if c + 1 < NCH:
                    emitLoadXbf(c + 1)
                    emitS1(c + 1)
                emitRep2(c)
                if c > 0:
                    emitY2grp(c - 1, [2, 3], c)
                emitLn2(c)
                if c > 0:
                    emitOuts(c - 1, [2, 3])
                if c > 0:
                    emitY2grp(c - 1, [4, 5], c)
                    emitOuts(c - 1, [4, 5])
                emitScores(c)
                emitSum(c)
                emitSumInv(c)
                if c > 0:
                    emitY2grp(c - 1, [6, 7], c)
                    emitOuts(c - 1, [6, 7])
                    emitStore(c - 1)
                emitE1(c)
                if c + 1 < NCH:
                    emitPairs2(c + 1, c)
                    emitH1g(c, E - 1)
            emitY2grp(NCH - 1, list(range(E)), None)
            emitOuts(NCH - 1, list(range(E)))
            emitStore(NCH - 1)

            def _fin(sync):
                sync.wait_ge(s_st.h, s_st.n)
            on("sync", _fin)

        all_sems = ([s_lds, s_ldx, s_ldw, s_ldxb, s_ldxh, s_st, s_mm, s_dve,
                     s_act, s_gp] + s_wq)

        def _reset_sems():
            for s in all_sems:
                s.n = 0

        mock = _MockEng()
        state["counting"] = True
        state["on"] = lambda eng, fn: fn(mock)
        _reset_sems()
        run_preamble()
        run_schedule()

        state["counting"] = False
        for target in ("sync", "gpsimd", "scalar", "vector", "tensor"):
            fns = []
            state["on"] = (lambda t: (lambda eng, fn: fns.append(fn)
                                      if eng == t else None))(target)
            _reset_sems()
            run_preamble()
            run_schedule()

            def body(engobj, fns=fns):
                for f in fns:
                    f(engobj)
            getattr(block, target)(body)

    return nc


_NC_CACHE = {}


def _get_nc(T, TC):
    key = (T, TC)
    if key not in _NC_CACHE:
        _NC_CACHE[key] = build_program(T, TC)
    return _NC_CACHE[key]


def _pad_wg(Wg):
    wg128 = np.zeros((128, KC, 128), dtype=np.float32)
    wg128[:, :, 0:E] = Wg.reshape(KC, 128, E).transpose(1, 0, 2)
    return np.ascontiguousarray(wg128.reshape(128, KC * 128))


def kernel(L0_attn_x, L0_attn_w_qkva, L0_attn_gamma, L0_ffn_W1, L0_ffn_b1,
           L0_ffn_W2, L0_ffn_b2, L0_ffn_Wg, L0_ffn_bg):
    B, T, _ = L0_attn_x.shape
    x = np.asarray(L0_attn_x, dtype=np.float32)
    wq = np.asarray(L0_attn_w_qkva, dtype=np.float32)
    gamma = np.asarray(L0_attn_gamma, dtype=np.float32)
    W1 = np.asarray(L0_ffn_W1, dtype=np.float32)
    b1 = np.asarray(L0_ffn_b1, dtype=np.float32)
    W2 = np.asarray(L0_ffn_W2, dtype=np.float32)
    b2 = np.asarray(L0_ffn_b2, dtype=np.float32)
    Wg = np.asarray(L0_ffn_Wg, dtype=np.float32)
    bgv = np.asarray(L0_ffn_bg, dtype=np.float32)

    shared = {
        "wqT": np.ascontiguousarray(wq.T),
        "gamT": np.ascontiguousarray(gamma.reshape(KC, 128).T),
        "w1p": np.ascontiguousarray(W1.transpose(1, 0, 2).reshape(D, E * H)),
        "b1T": np.ascontiguousarray(b1.T),
        "w2p": np.ascontiguousarray(W2.reshape(E * H, D)),
        "b2": np.ascontiguousarray(b2),
        "wgp": _pad_wg(Wg),
        "bg": np.ascontiguousarray(bgv.reshape(E, 1)),
        "selc": np.kron(np.eye(E, dtype=np.float32),
                        np.ones((1, 128), dtype=np.float32)),
    }
    in_maps = []
    for b in range(B):
        m = dict(shared)
        m["xT"] = np.ascontiguousarray(x[b].T)
        in_maps.append(m)

    nc = _get_nc(T, 512)
    res = run_bass_kernel_spmd(nc, in_maps, core_ids=list(range(B)))
    out = np.empty((B, T, D), dtype=np.float32)
    for b in range(B):
        out[b] = res.results[b]["outT"].T
    return out


# version 62
# speedup vs baseline: 1.0391x; 1.0391x over previous
import numpy as np
import concourse.bass as bass
import concourse.mybir as mybir
from concourse.bass_utils import run_bass_kernel_spmd

F32 = mybir.dt.float32
BF16 = mybir.dt.bfloat16
ALU = mybir.AluOpType
AF = mybir.ActivationFunctionType

D = 1024
KC = 8
E = 8
H = 128


class Sem:
    def __init__(self, stack, nc, name):
        self.h = stack.enter_context(nc.semaphore(name))
        self.n = 0

    def inc(self, instr, amt=1):
        instr.then_inc(self.h, amt)
        self.n += amt
        return self.n


def build_program(T=2048, TC=512):
    NCH = T // TC
    nc = bass.Bass("TRN2", target_bir_lowering=False)

    xT = nc.dram_tensor("xT", [D, T], F32, kind="ExternalInput").ap()
    wqT = nc.dram_tensor("wqT", [D, 3 * D], F32, kind="ExternalInput").ap()
    gamT = nc.dram_tensor("gamT", [128, KC], F32, kind="ExternalInput").ap()
    w1p = nc.dram_tensor("w1p", [D, E * H], F32, kind="ExternalInput").ap()
    b1T = nc.dram_tensor("b1T", [H, E], F32, kind="ExternalInput").ap()
    w2p = nc.dram_tensor("w2p", [E * H, D], F32, kind="ExternalInput").ap()
    b2 = nc.dram_tensor("b2", [E, D], F32, kind="ExternalInput").ap()
    wgp = nc.dram_tensor("wgp", [128, KC * 128], F32, kind="ExternalInput").ap()
    bg = nc.dram_tensor("bg", [E, 1], F32, kind="ExternalInput").ap()
    selc = nc.dram_tensor("selc", [E, E * 128], F32, kind="ExternalInput").ap()
    outT = nc.dram_tensor("outT", [D, T], F32, kind="ExternalOutput").ap()

    xT_v = xT.rearrange("(a p) t -> p a t", p=128)
    outT_v = outT.rearrange("(a p) t -> p a t", p=128)
    wqT_v = wqT.rearrange("(a p) e -> p a e", p=128)
    w1p_v = w1p.rearrange("(a p) f -> p a f", p=128)
    w2p_v = w2p.rearrange("(e p) d -> p e d", p=128)

    import contextlib
    stack = contextlib.ExitStack()
    with stack:
        sb = lambda name, shape, dt: stack.enter_context(nc.sbuf_tensor(name, shape, dt))
        wq_bf = sb("wq_bf", [128, KC, 3 * D], BF16)
        w1_bf = sb("w1_bf", [128, KC, E * H], BF16)
        w2_bf = sb("w2_bf", [128, E, D], BF16)
        wg_bf = sb("wg_bf", [128, KC, 128], BF16)
        b2_bf = sb("b2_bf", [E, D], BF16)
        gam = sb("gam", [128, KC], F32)
        b1s = sb("b1s", [H, E], F32)
        bgs = sb("bgs", [E, 1], F32)
        ones_bf = sb("ones_bf", [128, 128], BF16)
        ones_f = sb("ones_f", [128, 128], F32)
        statc = sb("statc", [128, 1], BF16)
        sel_bf = sb("sel_bf", [E, E * 128], BF16)

        x_sb = sb("x_sb", [128, KC, TC], F32)
        xbf = sb("xbf", [128, KC, TC], BF16)
        rm_bf = sb("rm_bf", [128, 2, TC], BF16)
        rm2_bf = sb("rm2_bf", [128, 2, TC], BF16)
        tbf = sb("tbf", [128, KC, TC], BF16)
        xsq = tbf
        hbf = sb("hbf", [128, KC, TC], BF16)
        y_sb = sb("y_sb", [128, KC, TC], F32)
        x2_sb = sb("x2_sb", [128, KC, TC], F32)
        x2_sb2 = sb("x2_sb2", [128, KC, TC], F32)
        asig = sb("asig", [128, 2, TC], F32)
        h1g = sb("h1g", [128, E, TC], BF16)
        carry = sb("carry", [128, KC], F32)
        rows = sb("rows", [1, 5 * TC], F32)
        plscr = sb("plscr", [1, 1], F32)
        rows_bf = sb("rows_bf", [1, 2 * TC], BF16)
        exg = sb("exg", [8, TC], BF16)
        g_bf = sb("g_bf", [8, TC], BF16)

        def x2buf(c):
            return x2_sb if c % 2 == 0 else x2_sb2

        def row(i):
            return rows[0:1, i * TC:(i + 1) * TC]
        R_M, R_T0, R_S2, R_INV, R_GI = 0, 1, 2, 3, 4

        def rowbf(i):
            return rows_bf[0:1, i * TC:(i + 1) * TC]

        ps = lambda name, shape: stack.enter_context(nc.psum_tensor(name, shape, F32))
        ps_kv = ps("ps_kv", [128, 2, TC])
        ps_a = ps("ps_a", [128, 2, TC])
        ps_rep = ps("ps_rep", [128, 2, TC])
        ps_q = ps("ps_q", [128, 2, TC])

        s_lds = Sem(stack, nc, "s_lds")
        s_wq = [Sem(stack, nc, f"s_wq{g}") for g in range(3)]
        s_ldx = Sem(stack, nc, "s_ldx")
        s_ldw = Sem(stack, nc, "s_ldw")
        s_ldxb = Sem(stack, nc, "s_ldxb")
        s_ldxh = Sem(stack, nc, "s_ldxh")
        s_st = Sem(stack, nc, "s_st")
        s_mm = Sem(stack, nc, "s_mm")
        s_dve = Sem(stack, nc, "s_dve")
        s_act = Sem(stack, nc, "s_act")
        s_gp = Sem(stack, nc, "s_gp")

        block = stack.enter_context(nc.Block())

        class _MockInstr:
            def then_inc(self, sem, amt=1):
                return self

        class _MockEng:
            def __getattr__(self, name):
                def f(*a, **k):
                    return _MockInstr()
                return f

        state = {"counting": True, "on": None}

        def on(eng, fn):
            state["on"](eng, fn)

        def rec(d, key, sem):
            if state["counting"]:
                d[key] = sem.n

        def rec_i(lst, i, sem):
            if state["counting"]:
                lst[i] = sem.n

        def init(d, key):
            if state["counting"]:
                d[key] = [0] * KC

        ck = [dict() for _ in range(NCH)]

        G = {}

        def run_preamble():
            def _pre_sync(sync):
                sync.dma_start(out=gam[:], in_=gamT).then_inc(s_lds.h, 16)
                sync.dma_start(out=b1s[:], in_=b1T).then_inc(s_lds.h, 16)
                sync.dma_start(out=bgs[:], in_=bg).then_inc(s_lds.h, 16)
                sync.dma_start(out=x_sb[:], in_=xT_v[:, :, 0:TC]).then_inc(s_ldx.h, 16)
            on("sync", _pre_sync)
            s_lds.n += 48
            s_ldx.n += 16
            if state["counting"]:
                G["ld_small"] = s_lds.n
                ck[0]["ld_x"] = s_ldx.n

            def _pre_gp(gpsimd):
                gpsimd.dma_start(out=xbf[:, 0:4, :],
                                 in_=xT_v[:, 0:4, 0:TC]).then_inc(s_ldxh.h, 16)
                gpsimd.dma_start(out=xbf[:, 4:8, :],
                                 in_=xT_v[:, 4:8, 0:TC]).then_inc(s_ldxb.h, 16)
                for g, lo in ((1, 1024), (2, 2048), (0, 0)):
                    gpsimd.dma_start(out=wq_bf[:, :, lo:lo + 1024],
                                     in_=wqT_v[:, :, lo:lo + 1024]).then_inc(s_wq[g].h, 16)
                gpsimd.dma_start(out=w1_bf[:], in_=w1p_v).then_inc(s_ldw.h, 16)
                gpsimd.dma_start(out=w2_bf[:], in_=w2p_v).then_inc(s_ldw.h, 16)
                gpsimd.dma_start(out=wg_bf[:], in_=wgp).then_inc(s_ldw.h, 16)
                gpsimd.dma_start(out=b2_bf[:], in_=b2).then_inc(s_ldw.h, 16)
                gpsimd.dma_start(out=sel_bf[:], in_=selc).then_inc(s_ldw.h, 16)
            on("gpsimd", _pre_gp)
            s_ldxb.n += 16
            s_ldxh.n += 16
            for g in range(3):
                s_wq[g].n += 16
            s_ldw.n += 16 * 5
            if state["counting"]:
                ck[0]["ld_xbf_h"] = s_ldxh.n
                ck[0]["ld_xbf"] = s_ldxb.n
                G["ld_w"] = s_ldw.n

            def _pre_dve(vector):
                vector.memset(statc[:], 1.0 / D)
                vector.memset(plscr[:], 1.0)
                vector.memset(ones_bf[:], 1.0)
                ii = vector.memset(ones_f[:], 1.0)
                s_dve.inc(ii)
                vector.wait_ge(s_lds.h, G["ld_small"])
            on("vector", _pre_dve)
            if state["counting"]:
                G["memset"] = s_dve.n

        def emitFolds(g):
            def _fold(vector, g=g):
                vector.wait_ge(s_wq[g].h, 16)
                for k in range(KC):
                    ii = vector.tensor_scalar_mul(
                        wq_bf[:, k, g * 1024:(g + 1) * 1024],
                        wq_bf[:, k, g * 1024:(g + 1) * 1024],
                        gam[:, k:k + 1])
                s_dve.inc(ii)
            on("vector", _fold)
            if state["counting"]:
                G[f"fold{g}"] = s_dve.n

        def emit_preload(func):
            def _pl(scalar, func=func):
                scalar.wait_ge(s_act.h, s_act.n)
                ii = scalar.activation(plscr[:], plscr[:], func)
                s_act.inc(ii)
            on("scalar", _pl)

        def emit_rows(K, key, eps_add, key_m=None):
            def _rows(vector):
                vector.wait_ge(s_mm.h, K[key_m or key])
                s_dve.inc(vector.tensor_copy(row(R_M), ps_q[0:1, 0, :]))
                rec(K, key + "_rdm", s_dve)
                vector.wait_ge(s_dve.h, s_dve.n)
                s_dve.inc(vector.tensor_copy(rowbf(0), row(R_M)))
                s_dve.inc(vector.tensor_mul(row(R_T0), row(R_M), row(R_M)))
                vector.wait_ge(s_dve.h, s_dve.n)
                vector.wait_ge(s_mm.h, K[key])
                s_dve.inc(vector.scalar_tensor_tensor(row(R_S2), row(R_T0), -1.0,
                                                      ps_q[0:1, 1, :], ALU.mult,
                                                      ALU.add))
                rec(K, key + "_rd", s_dve)
                vector.wait_ge(s_dve.h, s_dve.n)
                if eps_add is None:
                    s_dve.inc(vector.tensor_scalar_max(row(R_S2), row(R_S2), 1e-33))
                else:
                    s_dve.inc(vector.tensor_scalar_add(row(R_S2), row(R_S2), eps_add))
                vector.wait_ge(s_dve.h, s_dve.n)
                ii = vector.reciprocal(row(R_INV), row(R_S2))
                s_dve.inc(ii)
            on("vector", _rows)
            rec(K, key + "_row1", s_dve)

            def _r(scalar):
                scalar.wait_ge(s_dve.h, K[key + "_row1"])
                ii = scalar.activation(rowbf(1), row(R_INV), AF.Sqrt)
                s_act.inc(ii)
            on("scalar", _r)
            rec(K, key + "_actr", s_act)

        def emitLoadX(c):
            K = ck[c]
            tsl = slice(c * TC, (c + 1) * TC)

            def _ld(sync):
                sync.wait_ge(s_gp.h, ck[c - 1]["x2"][KC - 1])
                sync.dma_start(out=x_sb[:], in_=xT_v[:, :, tsl]).then_inc(s_ldx.h, 16)
            on("sync", _ld)
            s_ldx.n += 16
            rec(K, "ld_x", s_ldx)

        def emitLoadXbf(c):
            K = ck[c]
            tsl = slice(c * TC, (c + 1) * TC)

            def _ldbf(gpsimd):
                gpsimd.wait_ge(s_dve.h, ck[c - 1]["dve_h"])
                if c == 1:
                    gpsimd.wait_ge(s_act.h, ck[0]["act_sq0"])
                else:
                    gpsimd.wait_ge(s_gp.h, ck[c - 1]["gp_sq"])
                gpsimd.dma_start(out=xbf[:], in_=xT_v[:, :, tsl]).then_inc(s_ldxb.h, 16)
            on("gpsimd", _ldbf)
            s_ldxb.n += 16
            rec(K, "ld_xbf", s_ldxb)

        def emitS1(c):
            K = ck[c]

            def _s1(tensor):
                if c == 0:
                    tensor.wait_ge(s_dve.h, G["memset"])
                else:
                    tensor.wait_ge(s_ldxb.h, K["ld_xbf"])
                    tensor.wait_ge(s_dve.h, ck[c - 1]["mm_s2b_rdm"])
                for k in range(KC):
                    if c == 0:
                        if k < 4:
                            tensor.wait_ge(s_ldxh.h, K["ld_xbf_h"])
                        else:
                            tensor.wait_ge(s_ldxb.h, K["ld_xbf"])
                    ii = tensor.matmul(ps_q[0:1, 0, :], statc[:], xbf[:, k, :],
                                       start=(k == 0), stop=(k == KC - 1))
                s_mm.inc(ii)
            on("tensor", _s1)
            rec(K, "mm_s1", s_mm)

        def emitSq(c):
            K = ck[c]
            if c == 0:
                def _sq0(scalar):
                    scalar.wait_ge(s_ldxh.h, K["ld_xbf_h"])
                    s_act.inc(scalar.activation(xsq[:, 0:4, :], xbf[:, 0:4, :],
                                                AF.Square))
                    scalar.wait_ge(s_ldxb.h, K["ld_xbf"])
                    ii = scalar.activation(xsq[:, 4:8, :], xbf[:, 4:8, :],
                                           AF.Square)
                    s_act.inc(ii)
                on("scalar", _sq0)
                if state["counting"]:
                    K["act_sq0h"] = s_act.n - 1
                rec(K, "act_sq0", s_act)
                return

            def _sq(gpsimd):
                gpsimd.wait_ge(s_ldxb.h, K["ld_xbf"])
                gpsimd.wait_ge(s_dve.h, ck[c - 1]["dve_h"])
                gpsimd.wait_ge(s_mm.h, ck[c - 1]["mm_s2b"])
                ii = gpsimd.tensor_tensor(xsq[:], xbf[:], xbf[:], ALU.mult)
                s_gp.inc(ii)
            on("gpsimd", _sq)
            rec(K, "gp_sq", s_gp)

        def emitS2(c):
            K = ck[c]

            def _s2(tensor):
                if c != 0:
                    tensor.wait_ge(s_gp.h, K["gp_sq"])
                    tensor.wait_ge(s_dve.h, ck[c - 1]["dve_invs"])
                for k in range(KC):
                    if c == 0:
                        tensor.wait_ge(s_act.h,
                                       K["act_sq0h"] if k < 4 else K["act_sq0"])
                    ii = tensor.matmul(ps_q[0:1, 1, :], statc[:], xsq[:, k, :],
                                       start=(k == 0), stop=(k == KC - 1))
                s_mm.inc(ii)
            on("tensor", _s2)
            rec(K, "mm_s2", s_mm)

        def emitRows1(c, pl=True):
            emit_rows(ck[c], "mm_s2", eps_add=None, key_m="mm_s1")
            if pl:
                emit_preload(AF.Sigmoid)

        def emitRep1(c):
            K = ck[c]

            def _rep(tensor):
                tensor.wait_ge(s_act.h, K["mm_s2_actr"])
                tensor.matmul(ps_q[:, 0, :], ones_bf[0:1, :], rowbf(1),
                              start=True, stop=True)
                ii = tensor.matmul(ps_q[:, 1, :], ones_bf[0:1, :], rowbf(0),
                                   start=True, stop=True)
                s_mm.inc(ii)
            on("tensor", _rep)
            rec(K, "mm_rep1", s_mm)

            def _ev(scalar):
                scalar.wait_ge(s_mm.h, K["mm_rep1"])
                ii = scalar.activation(rm_bf[:, :, :], ps_q[:, :, :], AF.Copy)
                s_act.inc(ii)
            on("scalar", _ev)
            rec(K, "evac1", s_act)

        def emit_h_mul(c, ks, first):
            K = ck[c]

            def _m(vector):
                if first:
                    vector.wait_ge(s_act.h, K["evac1"])
                    vector.wait_ge(s_ldxb.h, K["ld_xbf"])
                for k in ks:
                    ii = vector.tensor_tensor(tbf[:, k, :], xbf[:, k, :],
                                              rm_bf[:, 1, :], ALU.subtract)
                    s_dve.inc(ii)
            on("vector", _m)

        def emitPairs2(c, cprev):
            K = ck[c]
            init(K, "pair")

            def _s(vector):
                vector.wait_ge(s_dve.h, s_dve.n)
                for k in range(KC):
                    if cprev is not None:
                        vector.wait_ge(s_mm.h, ck[cprev]["h1k"][k])
                    ii = vector.tensor_tensor(hbf[:, k, :], tbf[:, k, :],
                                              rm_bf[:, 0, :], ALU.mult)
                    s_dve.inc(ii)
                    rec_i(K["pair"], k, s_dve)
            on("vector", _s)
            rec(K, "dve_h", s_dve)

        def emitB(c):
            K = ck[c]
            init(K, "mm_kv")
            init(K, "mm_a")
            init(K, "sig")
            init(K, "scan")
            init(K, "mm_q")
            init(K, "attn")
            init(K, "x2")

            for j in range(KC):
                def _kv(tensor, j=j):
                    if j == 0:
                        if c == 0:
                            tensor.wait_ge(s_dve.h, G["fold1"])
                        if c > 0:
                            tensor.wait_ge(s_act.h, ck[c - 1]["relu"][6])
                    if j == 1 and c > 0:
                        tensor.wait_ge(s_act.h, ck[c - 1]["relu"][7])
                    if j >= 2:
                        tensor.wait_ge(s_dve.h, K["scan"][j - 2])
                    for k in range(KC):
                        if j == 0:
                            tensor.wait_ge(s_dve.h, K["pair"][k])
                        ii = tensor.matmul(ps_kv[:, j % 2, :],
                                           wq_bf[:, k, (8 + j) * 128:(9 + j) * 128],
                                           hbf[:, k, :],
                                           start=(k == 0), stop=(k == KC - 1))
                    s_mm.inc(ii)
                on("tensor", _kv)
                rec_i(K["mm_kv"], j, s_mm)

                def _a(tensor, j=j):
                    if j == 0 and c == 0:
                        tensor.wait_ge(s_dve.h, G["fold2"])
                    if j == 0 and c > 0:
                        tensor.wait_ge(s_dve.h, ck[c - 1]["dve_g"])
                        tensor.wait_ge(s_act.h, ck[c - 1]["relu"][4])
                    if j == 1 and c > 0:
                        tensor.wait_ge(s_act.h, ck[c - 1]["relu"][5])
                    if j >= 2:
                        tensor.wait_ge(s_act.h, K["sig"][j - 2])
                    for k in range(KC):
                        ii = tensor.matmul(ps_a[:, j % 2, :],
                                           wq_bf[:, k, (16 + j) * 128:(17 + j) * 128],
                                           hbf[:, k, :],
                                           start=(k == 0), stop=(k == KC - 1))
                    s_mm.inc(ii)
                on("tensor", _a)
                rec_i(K["mm_a"], j, s_mm)

                def _sig(scalar, j=j):
                    scalar.wait_ge(s_mm.h, K["mm_a"][j])
                    if c > 0 and j < 2:
                        scalar.wait_ge(s_gp.h, ck[c - 1]["gp_ln2"])
                    ii = scalar.activation(asig[:, j % 2, :], ps_a[:, j % 2, :],
                                           AF.Sigmoid)
                    s_act.inc(ii)
                on("scalar", _sig)
                rec_i(K["sig"], j, s_act)
                if j == KC - 1:
                    emit_preload(AF.Sqrt)

                def _scan(vector, j=j):
                    vector.wait_ge(s_act.h, K["sig"][j])
                    init = 0.0 if c == 0 else carry[:, j:j + 1]
                    s_dve.inc(vector.tensor_tensor_scan(y_sb[:, j, :], asig[:, j % 2, :],
                                                        ps_kv[:, j % 2, :], init,
                                                        ALU.mult, ALU.add))
                    vector.wait_ge(s_dve.h, s_dve.n)
                    ii = vector.tensor_copy(carry[:, j:j + 1], y_sb[:, j, TC - 1:TC])
                    s_dve.inc(ii)
                on("vector", _scan)
                rec_i(K["scan"], j, s_dve)

                if j >= 2:
                    emit_q(c, j - 2)
            emit_q(c, KC - 2)
            emit_q(c, KC - 1)

        def emit_q(c, j):
            K = ck[c]
            if True:
                def _q(tensor, j=j):
                    if j < 2:
                        tensor.wait_ge(s_dve.h, K["dve_h"])
                        tensor.wait_ge(s_act.h, K["evac1"])
                        if c == 0:
                            tensor.wait_ge(s_dve.h, G["fold0"])
                    if j >= 2:
                        tensor.wait_ge(s_dve.h, K["attn"][j - 2])
                    for k in range(KC):
                        ii = tensor.matmul(ps_q[:, j % 2, :],
                                           wq_bf[:, k, j * 128:(j + 1) * 128],
                                           hbf[:, k, :],
                                           start=(k == 0), stop=(k == KC - 1))
                    s_mm.inc(ii)
                on("tensor", _q)
                rec_i(K["mm_q"], j, s_mm)

                def _attn(vector, j=j):
                    vector.wait_ge(s_mm.h, K["mm_q"][j])
                    vector.wait_ge(s_dve.h, K["scan"][j])
                    ii = vector.tensor_tensor(y_sb[:, j, :], ps_q[:, j % 2, :],
                                              y_sb[:, j, :], ALU.mult)
                    s_dve.inc(ii)
                on("vector", _attn)
                rec_i(K["attn"], j, s_dve)

                def _x2(gpsimd, j=j):
                    gpsimd.wait_ge(s_dve.h, K["attn"][j])
                    if j == 0:
                        gpsimd.wait_ge(s_ldx.h, K["ld_x"])
                    if j == 0 and c > 1:
                        gpsimd.wait_ge(s_st.h, ck[c - 2]["st"])
                    ii = gpsimd.tensor_tensor(x2buf(c)[:, j, :], x_sb[:, j, :],
                                              y_sb[:, j, :], ALU.add)
                    s_gp.inc(ii)
                on("gpsimd", _x2)
                rec_i(K["x2"], j, s_gp)

        def emitC(c):
            K = ck[c]
            init(K, "cast")

            for j in range(KC):
                def _cast(scalar, j=j):
                    scalar.wait_ge(s_gp.h, K["x2"][j])
                    if j == 0:
                        scalar.wait_ge(s_mm.h, K["mm_q"][KC - 1])
                    ii = scalar.activation(hbf[:, j, :], x2buf(c)[:, j, :], AF.Copy)
                    s_act.inc(ii)
                on("scalar", _cast)
                rec_i(K["cast"], j, s_act)

            init(K, "mm_s1bk")
            init(K, "sq2")

            def _s1(tensor):
                tensor.wait_ge(s_dve.h, K["attn"][KC - 1])
                for k in range(KC):
                    tensor.wait_ge(s_act.h, K["cast"][k])
                    ii = tensor.matmul(ps_q[0:1, 0, :], statc[:], hbf[:, k, :],
                                       start=(k == 0), stop=(k == KC - 1))
                    s_mm.inc(ii)
                    rec_i(K["mm_s1bk"], k, s_mm)
            on("tensor", _s1)
            rec(K, "mm_s1b", s_mm)

            def _sq(scalar):
                for k in range(KC):
                    scalar.wait_ge(s_mm.h, K["mm_s1bk"][k])
                    ii = scalar.activation(tbf[:, k, :], hbf[:, k, :], AF.Square)
                    s_act.inc(ii)
                    rec_i(K["sq2"], k, s_act)
            on("scalar", _sq)

            def _s2(tensor):
                for k in range(KC):
                    tensor.wait_ge(s_act.h, K["sq2"][k])
                    ii = tensor.matmul(ps_q[0:1, 1, :], statc[:], tbf[:, k, :],
                                       start=(k == 0), stop=(k == KC - 1))
                s_mm.inc(ii)
            on("tensor", _s2)
            rec(K, "mm_s2b", s_mm)

            emit_rows(K, "mm_s2b", eps_add=1e-5, key_m="mm_s1b")
            emit_preload(AF.Exp)

        def emitRep2(c):
            K = ck[c]

            def _rep(tensor):
                tensor.wait_ge(s_act.h, K["mm_s2b_actr"])
                if c > 0:
                    tensor.wait_ge(s_dve.h, ck[c - 1]["h1g"][E - 1])
                tensor.matmul(ps_rep[:, 0, :], ones_bf[0:1, :], rowbf(1),
                              start=True, stop=True)
                ii = tensor.matmul(ps_rep[:, 1, :], ones_bf[0:1, :], rowbf(0),
                                   start=True, stop=True)
                s_mm.inc(ii)
            on("tensor", _rep)
            rec(K, "mm_rep2", s_mm)

            def _evg(vector):
                vector.wait_ge(s_mm.h, K["mm_rep2"])
                vector.tensor_copy(asig[:, 0, :], ps_rep[:, 0, :])
                ii = vector.tensor_copy(asig[:, 1, :], ps_rep[:, 1, :])
                s_dve.inc(ii)
            on("vector", _evg)
            rec(K, "dve_evac", s_dve)

        def emitLn2(c):
            K = ck[c]
            init(K, "ln2k")
            NDVE = 5
            for k in range(NDVE):
                def _ln2(vector, k=k):
                    s_dve.inc(vector.tensor_tensor(y_sb[:, k, :], x2buf(c)[:, k, :],
                                                   ps_rep[:, 1, :], ALU.subtract))
                    vector.wait_ge(s_dve.h, s_dve.n)
                    ii = vector.tensor_tensor(hbf[:, k, :], y_sb[:, k, :],
                                              ps_rep[:, 0, :], ALU.mult)
                    s_dve.inc(ii)
                on("vector", _ln2)
                rec_i(K["ln2k"], k, s_dve)
            for k in range(NDVE, KC):
                def _ln2g(gpsimd, k=k):
                    if k == NDVE:
                        gpsimd.wait_ge(s_dve.h, K["dve_evac"])
                    s_gp.inc(gpsimd.tensor_tensor(y_sb[:, k, :], x2buf(c)[:, k, :],
                                                  asig[:, 1, :], ALU.subtract))
                    gpsimd.wait_ge(s_gp.h, s_gp.n)
                    ii = gpsimd.tensor_tensor(hbf[:, k, :], y_sb[:, k, :],
                                              asig[:, 0, :], ALU.mult)
                    s_gp.inc(ii)
                on("gpsimd", _ln2g)
                rec_i(K["ln2k"], k, s_gp)
            if state["counting"]:
                K["dve_ln2"] = K["ln2k"][NDVE - 1]
                K["gp_ln2"] = K["ln2k"][KC - 1]
            K["n_dve_ln2"] = NDVE

        def emitScores(c):
            K = ck[c]

            def _sc(tensor):
                if c == 0:
                    tensor.wait_ge(s_ldw.h, G["ld_w"])
                if c > 0:
                    tensor.wait_ge(s_dve.h, ck[c - 1]["out"][2])
                for k in range(KC):
                    if k < K["n_dve_ln2"]:
                        tensor.wait_ge(s_dve.h, K["ln2k"][k])
                    else:
                        tensor.wait_ge(s_gp.h, K["ln2k"][k])
                    ii = tensor.matmul(ps_a[:, 0, :], wg_bf[:, k, :], hbf[:, k, :],
                                       start=(k == 0), stop=(k == KC - 1))
                s_mm.inc(ii)
            on("tensor", _sc)
            rec(K, "mm_sc", s_mm)

            def _exp(scalar):
                scalar.wait_ge(s_mm.h, K["mm_sc"])
                ii = scalar.activation(exg[:], ps_a[0:8, 0, :], AF.Exp, bias=bgs[:])
                s_act.inc(ii)
            on("scalar", _exp)
            rec(K, "act_exp", s_act)

        def emitSum(c):
            K = ck[c]

            def _sum(tensor):
                tensor.wait_ge(s_act.h, K["act_exp"])
                ii = tensor.matmul(ps_q[0:1, 1, :], ones_bf[0:8, 0:1], exg[:],
                                   start=True, stop=True)
                s_mm.inc(ii)
            on("tensor", _sum)
            rec(K, "mm_sum", s_mm)

        def emitSumInv(c):
            K = ck[c]

            def _inv(vector):
                vector.wait_ge(s_mm.h, K["mm_sum"])
                ii = vector.reciprocal(row(R_GI), ps_q[0:1, 1, :])
                s_dve.inc(ii)
            on("vector", _inv)
            rec(K, "dve_invs", s_dve)

        def emitDtail(c):
            K = ck[c]

            def _invrep(tensor):
                tensor.wait_ge(s_dve.h, K["dve_invs"])
                if c > 0:
                    tensor.wait_ge(s_dve.h, ck[c - 1]["out"][E - 1])
                ii = tensor.matmul(ps_a[0:8, 1, :], ones_f[0:1, 0:8], row(R_GI),
                                   start=True, stop=True)
                s_mm.inc(ii)
            on("tensor", _invrep)
            rec(K, "mm_invr", s_mm)

            def _g(vector):
                vector.wait_ge(s_mm.h, K["mm_invr"])
                if c > 0:
                    vector.wait_ge(s_mm.h, ck[c - 1]["mm_y2"][E - 1])
                ii = vector.tensor_tensor(g_bf[:], exg[:], ps_a[0:8, 1, :], ALU.mult)
                s_dve.inc(ii)
            on("vector", _g)
            rec(K, "dve_g", s_dve)
            emit_preload(AF.Sqrt)

        def emitE1(c):
            K = ck[c]
            init(K, "mm_h1")
            init(K, "h1k")
            init(K, "mm_gr")
            init(K, "relu")
            init(K, "h1g")

            BM = (0, 1, 0, 1, 2, 3, 0, 1)

            for e in range(E):
                def _h1(tensor, e=e):
                    if e == 0:
                        if c > 0:
                            tensor.wait_ge(s_dve.h, ck[c - 1]["out"][4])
                    if e == 1 and c > 0:
                        tensor.wait_ge(s_dve.h, ck[c - 1]["out"][5])
                    if e == 2:
                        tensor.wait_ge(s_act.h, K["relu"][0])
                    if e == 3:
                        tensor.wait_ge(s_act.h, K["relu"][1])
                    if e == 4 and c > 0:
                        tensor.wait_ge(s_dve.h, ck[c - 1]["out"][6])
                    if e == 5 and c > 0:
                        tensor.wait_ge(s_dve.h, ck[c - 1]["out"][E - 1])
                    if e == 4:
                        tensor.wait_ge(s_act.h, K["act_exp"])
                    if e == 5:
                        tensor.wait_ge(s_dve.h, K["dve_g"])
                    if e == 6:
                        tensor.wait_ge(s_act.h, K["relu"][2])
                    if e == 7:
                        tensor.wait_ge(s_act.h, K["relu"][3])
                    pb = ps_kv if BM[e] < 2 else ps_a
                    for k in range(KC):
                        if e == 0:
                            if k < K["n_dve_ln2"]:
                                tensor.wait_ge(s_dve.h, K["ln2k"][k])
                            else:
                                tensor.wait_ge(s_gp.h, K["ln2k"][k])
                        ii = tensor.matmul(pb[:, BM[e] % 2, :],
                                           w1_bf[:, k, e * 128:(e + 1) * 128],
                                           hbf[:, k, :],
                                           start=(k == 0), stop=(k == KC - 1))
                        if e == E - 1:
                            s_mm.inc(ii)
                            rec_i(K["h1k"], k, s_mm)
                    if e != E - 1:
                        s_mm.inc(ii)
                on("tensor", _h1)
                rec_i(K["mm_h1"], e, s_mm)

                if e == 2:
                    emitDtail(c)

                    def _gr01(tensor):
                        tensor.wait_ge(s_dve.h, K["dve_g"])
                        for ee in range(2):
                            ii = tensor.matmul(ps_rep[:, ee % 2, :],
                                               sel_bf[:, ee * 128:(ee + 1) * 128],
                                               g_bf[:], start=True, stop=True)
                            s_mm.inc(ii)
                            rec_i(K["mm_gr"], ee, s_mm)
                    on("tensor", _gr01)

                    def _h1g01(vector):
                        if c > 0:
                            vector.wait_ge(s_mm.h, ck[c - 1]["mm_y2"][E - 1])
                        for ee in range(2):
                            vector.wait_ge(s_act.h, K["relu"][ee])
                            vector.wait_ge(s_mm.h, K["mm_gr"][ee])
                            ii = vector.tensor_tensor(h1g[:, ee, :],
                                                      y_sb[:, ee, :],
                                                      ps_rep[:, ee % 2, :],
                                                      ALU.mult)
                            s_dve.inc(ii)
                            rec_i(K["h1g"], ee, s_dve)
                    on("vector", _h1g01)

                    if c + 1 < NCH:
                        emitSq(c + 1)
                        emitS2(c + 1)
                        emitRows1(c + 1, pl=False)

                if e == 4 and c + 1 < NCH:
                    emitRep1(c + 1)

                if e >= 2:
                    def _gr(tensor, e=e):
                        tensor.wait_ge(s_dve.h, K["h1g"][e - 2])
                        ii = tensor.matmul(ps_rep[:, e % 2, :],
                                           sel_bf[:, e * 128:(e + 1) * 128],
                                           g_bf[:], start=True, stop=True)
                        s_mm.inc(ii)
                    on("tensor", _gr)
                    rec_i(K["mm_gr"], e, s_mm)

                def _relu(scalar, e=e):
                    scalar.wait_ge(s_mm.h, K["mm_h1"][e])
                    pb = ps_kv if BM[e] < 2 else ps_a
                    ii = scalar.activation(y_sb[:, e, :], pb[:, BM[e] % 2, :],
                                           AF.Relu, bias=b1s[:, e:e + 1])
                    s_act.inc(ii)
                on("scalar", _relu)
                rec_i(K["relu"], e, s_act)

                if e >= 5 and c + 1 < NCH:
                    ks = {5: (0, 1, 2), 6: (3, 4, 5), 7: (6, 7)}[e]
                    emit_h_mul(c + 1, ks, first=(e == 5))
                if e == 6 and c + 1 < NCH:
                    emit_preload(AF.Sigmoid)

                if e >= 2 and not (e == E - 1 and c + 1 < NCH):
                    emitH1g(c, e)

        def emitH1g(c, e):
            K = ck[c]

            def _h1g(vector, e=e):
                vector.wait_ge(s_act.h, K["relu"][e])
                vector.wait_ge(s_mm.h, K["mm_gr"][e])
                ii = vector.tensor_tensor(h1g[:, e, :], y_sb[:, e, :],
                                          ps_rep[:, e % 2, :], ALU.mult)
                s_dve.inc(ii)
            on("vector", _h1g)
            rec_i(K["h1g"], e, s_dve)

        def emitY2grp(c, ds, next_c):
            K = ck[c]
            if state["counting"]:
                K.setdefault("mm_y2", [0] * KC)
                K.setdefault("out", [0] * KC)

            for d in ds:
                def _y2(tensor, d=d):
                    if d == 0:
                        tensor.wait_ge(s_dve.h, K["h1g"][E - 1])
                        if next_c is not None:
                            tensor.wait_ge(s_dve.h, ck[next_c]["scan"][KC - 1])
                    if d >= 4:
                        tensor.wait_ge(s_dve.h, K["out"][d - 4])
                    pb = ps_kv if d % 4 < 2 else ps_a
                    tensor.matmul(pb[:, d % 2, :], b2_bf[:, d * 128:(d + 1) * 128],
                                  g_bf[:], start=True, stop=False)
                    for e in range(E):
                        ii = tensor.matmul(pb[:, d % 2, :],
                                           w2_bf[:, e, d * 128:(d + 1) * 128],
                                           h1g[:, e, :],
                                           start=False, stop=(e == E - 1))
                    s_mm.inc(ii)
                on("tensor", _y2)
                rec_i(K["mm_y2"], d, s_mm)

        def emitOuts(c, ds):
            K = ck[c]
            for d in ds:
                def _out(vector, d=d):
                    vector.wait_ge(s_mm.h, K["mm_y2"][d])
                    pb = ps_kv if d % 4 < 2 else ps_a
                    ii = vector.tensor_tensor(x2buf(c)[:, d, :], pb[:, d % 2, :],
                                              x2buf(c)[:, d, :], ALU.add)
                    s_dve.inc(ii)
                on("vector", _out)
                rec_i(K["out"], d, s_dve)

        def emitStore(c):
            K = ck[c]
            tsl = slice(c * TC, (c + 1) * TC)

            def _st(sync):
                sync.wait_ge(s_dve.h, K["out"][E - 1])
                sync.dma_start(out=outT_v[:, :, tsl], in_=x2buf(c)[:]).then_inc(s_st.h, 16)
            on("sync", _st)
            s_st.n += 16
            rec(K, "st", s_st)

        def run_schedule():
            emitS1(0)
            emitSq(0)
            emitS2(0)
            emitRows1(0)
            emitFolds(1)
            emitRep1(0)
            def _boot_pairs(vector):
                vector.wait_ge(s_act.h, ck[0]["evac1"])
                vector.wait_ge(s_ldxb.h, ck[0]["ld_xbf"])
                if state["counting"]:
                    ck[0]["pair"] = [0] * KC
                for k in range(KC):
                    s_dve.inc(vector.tensor_tensor(tbf[:, k, :], xbf[:, k, :],
                                                   rm_bf[:, 1, :], ALU.subtract))
                    vector.wait_ge(s_dve.h, s_dve.n)
                    ii = vector.tensor_tensor(hbf[:, k, :], tbf[:, k, :],
                                              rm_bf[:, 0, :], ALU.mult)
                    s_dve.inc(ii)
                    rec_i(ck[0]["pair"], k, s_dve)
            on("vector", _boot_pairs)
            rec(ck[0], "dve_h", s_dve)
            emitFolds(2)
            emitFolds(0)
            for c in range(NCH):
                emitB(c)
                if c + 1 < NCH:
                    emitLoadX(c + 1)
                emitC(c)
                if c > 0:
                    emitY2grp(c - 1, [0, 1], c)
                    emitOuts(c - 1, [0, 1])
                if c + 1 < NCH:
                    emitLoadXbf(c + 1)
                    emitS1(c + 1)
                emitRep2(c)
                if c > 0:
                    emitY2grp(c - 1, [2, 3], c)
                emitLn2(c)
                if c > 0:
                    emitOuts(c - 1, [2, 3])
                if c > 0:
                    emitY2grp(c - 1, [4, 5], c)
                    emitOuts(c - 1, [4, 5])
                emitScores(c)
                emitSum(c)
                emitSumInv(c)
                if c > 0:
                    emitY2grp(c - 1, [6, 7], c)
                    emitOuts(c - 1, [6, 7])
                    emitStore(c - 1)
                emitE1(c)
                if c + 1 < NCH:
                    emitPairs2(c + 1, c)
                    emitH1g(c, E - 1)
            emitY2grp(NCH - 1, list(range(E)), None)
            emitOuts(NCH - 1, list(range(E)))
            emitStore(NCH - 1)

            def _fin(sync):
                sync.wait_ge(s_st.h, s_st.n)
            on("sync", _fin)

        all_sems = ([s_lds, s_ldx, s_ldw, s_ldxb, s_ldxh, s_st, s_mm, s_dve,
                     s_act, s_gp] + s_wq)

        def _reset_sems():
            for s in all_sems:
                s.n = 0

        mock = _MockEng()
        state["counting"] = True
        state["on"] = lambda eng, fn: fn(mock)
        _reset_sems()
        run_preamble()
        run_schedule()

        state["counting"] = False
        for target in ("sync", "gpsimd", "scalar", "vector", "tensor"):
            fns = []
            state["on"] = (lambda t: (lambda eng, fn: fns.append(fn)
                                      if eng == t else None))(target)
            _reset_sems()
            run_preamble()
            run_schedule()

            def body(engobj, fns=fns):
                for f in fns:
                    f(engobj)
            getattr(block, target)(body)

    return nc


_NC_CACHE = {}


def _get_nc(T, TC):
    key = (T, TC)
    if key not in _NC_CACHE:
        _NC_CACHE[key] = build_program(T, TC)
    return _NC_CACHE[key]


def _pad_wg(Wg):
    wg128 = np.zeros((128, KC, 128), dtype=np.float32)
    wg128[:, :, 0:E] = Wg.reshape(KC, 128, E).transpose(1, 0, 2)
    return np.ascontiguousarray(wg128.reshape(128, KC * 128))


def kernel(L0_attn_x, L0_attn_w_qkva, L0_attn_gamma, L0_ffn_W1, L0_ffn_b1,
           L0_ffn_W2, L0_ffn_b2, L0_ffn_Wg, L0_ffn_bg):
    B, T, _ = L0_attn_x.shape
    x = np.asarray(L0_attn_x, dtype=np.float32)
    wq = np.asarray(L0_attn_w_qkva, dtype=np.float32)
    gamma = np.asarray(L0_attn_gamma, dtype=np.float32)
    W1 = np.asarray(L0_ffn_W1, dtype=np.float32)
    b1 = np.asarray(L0_ffn_b1, dtype=np.float32)
    W2 = np.asarray(L0_ffn_W2, dtype=np.float32)
    b2 = np.asarray(L0_ffn_b2, dtype=np.float32)
    Wg = np.asarray(L0_ffn_Wg, dtype=np.float32)
    bgv = np.asarray(L0_ffn_bg, dtype=np.float32)

    shared = {
        "wqT": np.ascontiguousarray(wq.T),
        "gamT": np.ascontiguousarray(gamma.reshape(KC, 128).T),
        "w1p": np.ascontiguousarray(W1.transpose(1, 0, 2).reshape(D, E * H)),
        "b1T": np.ascontiguousarray(b1.T),
        "w2p": np.ascontiguousarray(W2.reshape(E * H, D)),
        "b2": np.ascontiguousarray(b2),
        "wgp": _pad_wg(Wg),
        "bg": np.ascontiguousarray(bgv.reshape(E, 1)),
        "selc": np.kron(np.eye(E, dtype=np.float32),
                        np.ones((1, 128), dtype=np.float32)),
    }
    in_maps = []
    for b in range(B):
        m = dict(shared)
        m["xT"] = np.ascontiguousarray(x[b].T)
        in_maps.append(m)

    nc = _get_nc(T, 512)
    res = run_bass_kernel_spmd(nc, in_maps, core_ids=list(range(B)))
    out = np.empty((B, T, D), dtype=np.float32)
    for b in range(B):
        out[b] = res.results[b]["outT"].T
    return out
